# version 07 (final)
# baseline (speedup 1.0000x reference)
import numpy as np
import ml_dtypes
from contextlib import ExitStack

import concourse.bass as bass
import concourse.mybir as mybir
from concourse.bass_utils import run_bass_kernel_spmd

F32 = mybir.dt.float32
BF16 = mybir.dt.bfloat16
AF = mybir.ActivationFunctionType
ALU = mybir.AluOpType

HID = 4096
H = 32
KVH = 8
CH = 128
DC = 32
PAN = 1024
HPAN = 512
NPC = 8
NTT = 2
NPANEL = 4
T_OWN = 2048
LN_EPS = 1e-5
RMS_EPS = 1e-5
BIG_NEG = -1e30
ROPE_THETA = 10000.0


class SemC:
    def __init__(self, h):
        self.h = h
        self.n = 0

    def bump(self, k):
        self.n += k
        return self.n


def build_nc(level=9, npan=NPANEL, p7nc=8, p7ng=8, p7pre=6, p7v4=0, dbg=False, p7ser=0, dbg2=False):
    nc = bass.Bass("TRN2", target_bir_lowering=False, debug=False,
                   detect_race_conditions=False)

    def din(name, shape, dt=F32):
        return nc.dram_tensor(name, shape, dt, kind="ExternalInput").ap()

    x_tok = din("x_tok", [4096, HID])
    xT = din("xT", [128, DC, 4096])
    wkv_t = din("wkv_t", [16, 128, DC, 128])
    wq_t = din("wq_t", [32, 128, DC, 128])
    wg_t = din("wg_t", [32, 128, DC, 128])
    wfc_t = din("wfc_t", [32, 128, DC + 2, 128])
    wo_t = din("wo_t", [32, 128, DC, 128])
    a_t = din("a_t", [2, 128, DC, 128])
    win_t = din("win_t", [128, DC, 32])
    bias_t = din("bias_t", [128, 32])
    dtb_h = din("dtb", [32, 1])
    alog_h = din("alog", [32, 1])
    gnw_h = din("gnw", [128, 1])
    coef_h = din("coef", [16, 1])
    scal_h = din("scal", [16, 1])
    cos_h = din("cos_t", [128, 4096], BF16)
    sin_h = din("sin_t", [128, 4096], BF16)
    maskT_h = din("maskT", [128, 128])
    identb_h = din("identb", [128, 128], BF16)
    identf_h = din("identf", [128, 128])
    ones3b_h = din("ones3b", [3, 128], BF16)
    nones3b_h = din("nones3b", [3, 128], BF16)
    ones1f_h = din("ones1f", [1, 128])
    sel_h = din("sel", [2, 16, 128])
    fconst_h = din("fconst", [128, 4])

    out_t = nc.dram_tensor("out_t", [32, 128, T_OWN], F32, kind="ExternalOutput").ap()
    spk = "ExternalOutput" if dbg else "Internal"
    q_sp = nc.dram_tensor("q_sp", [2, 32, 128, PAN], BF16, kind=spk).ap()
    gate_sp = nc.dram_tensor("gate_sp", [2, 32, 128, PAN], BF16, kind=spk).ap()
    g_sp = nc.dram_tensor("g_sp", [2, 32, 128, PAN], BF16, kind=spk).ap()
    if dbg:
        dbg_h = nc.dram_tensor("dbg_t", [128, 4, 32], F32, kind="ExternalOutput").ap()
        dbg2_h = nc.dram_tensor("dbg2_t", [32, 2, PAN], F32, kind="ExternalOutput").ap()
        ydbg_h = nc.dram_tensor("ydbg_t", [2, 2, 128, 32, HPAN], BF16, kind="ExternalOutput").ap()
    dbg3 = {}
    def dbg3_mk(name, shape, dt):
        dbg3[name] = nc.dram_tensor("dbg3_" + name, shape, dt, kind="ExternalOutput").ap()
    if dbg:
        dbg3_mk("mt_s", [128, 512], F32)
        dbg3_mk("et_s", [128, 512], BF16)
        dbg3_mk("std_s", [128, 512], BF16)
        dbg3_mk("t1_s", [128, 512], F32)
        dbg3_mk("yf_s", [128, 512], F32)
        dbg3_mk("ss4", [128, 4], F32)
        dbg3_mk("rinv4", [128, 4], F32)
        dbg3_mk("yn4", [128, 4, 128], BF16)
        dbg3_mk("xc4", [128, 4, 128], BF16)
        dbg3_mk("xcw4", [128, 4, 128], BF16)
        dbg3_mk("pv4", [128, 4, 128], BF16)
        dbg3_mk("dt_cT", [128, 32], F32)
        dbg3_mk("G_cT", [128, 32], F32)
        dbg3_mk("w_cT", [128, 32], F32)
        dbg3_mk("eG_cT", [128, 32], F32)
        dbg3_mk("cdec", [128, 32], F32)
        dbg3_mk("ktc", [128, 8, 128], BF16)
        dbg3_mk("glf", [1, 64], F32)
        dbg3_mk("state4", [128, 4, 128], F32)
        dbg3_mk("G_row", [32, PAN], F32)
        dbg3_mk("g_row", [32, PAN], F32)
        dbg3_mk("dt_row", [32, PAN], F32)
        dbg3_mk("acol", [32, 1], F32)
        dbg3_mk("spl_hi", [32, PAN], BF16)
        dbg3_mk("spl_mid", [32, PAN], BF16)
        dbg3_mk("spl_lo", [32, PAN], BF16)

    es = ExitStack()
    with es:
        def sb(name, shape, dt):
            return es.enter_context(nc.sbuf_tensor("sb_" + name, shape, dt))

        fconst = sb("fconst", [128, 4], F32)
        for _i, _val in enumerate((LN_EPS, -0.5, 1.0 / CH, 1.0 / HID)):
            nc.const_aps.register(F32, _val, fconst[:, _i:_i + 1])
        identb = sb("identb", [128, 128], BF16)
        identf = sb("identf", [128, 128], F32)
        maskT = sb("maskT", [128, 128], F32)
        ones3b = sb("ones3b", [3, 128], BF16)
        nones3b = sb("nones3b", [3, 128], BF16)
        ones1f = sb("ones1f", [1, 128], F32)
        sel_sb = sb("sel_sb", [16, 2, 128], F32)
        dtb = sb("dtb", [32, 1], F32)
        alog = sb("alog", [32, 1], F32)
        acol = sb("acol", [32, 1], F32)
        gnw = sb("gnw", [128, 1], F32)
        coef = sb("coef", [16, 1], F32)
        scal = sb("scal", [16, 1], F32)
        w16 = sb("w16", [16, 1], F32)
        barsc = sb("barsc", [1, 2], F32)
        wcol = sb("wcol", [128, 2], F32)
        bias_sb = sb("bias_sb", [128, 32], F32)
        win_b = sb("win_b", [128, DC, 32], BF16)
        cos_pan = sb("cos_pan", [128, PAN], BF16)
        sin_pan = sb("sin_pan", [128, PAN], BF16)
        mean_cols = sb("mean_cols", [128, 32], F32)
        r_cols = sb("r_cols", [128, 32], F32)
        st_sum = sb("st_sum", [128, 32], F32)
        st_ss = sb("st_ss", [128, 32], F32)
        state = sb("state", [128, H, 128], F32)
        wb = [sb(f"wb{i}", [128, DC + 2, 128], BF16) for i in range(2)]
        vt_sb = sb("vt_sb", [128, NPC, KVH, 128], BF16)
        krope = sb("krope", [128, KVH, PAN], BF16)
        dt_row = sb("dt_row", [32, PAN], F32)
        g_row = sb("g_row", [32, PAN], F32)
        G_row = sb("G_row", [32, PAN], F32)
        spl_hi = sb("spl_hi", [32, PAN], BF16)
        spl_mid = sb("spl_mid", [32, PAN], BF16)
        spl_lo = sb("spl_lo", [32, PAN], BF16)
        cpp = sb("cpp", [32, 128], F32)
        cumA = sb("cumA", [32, 192], F32)
        cumB = sb("cumB", [32, 192], F32)
        xa_sb = sb("xa_sb", [128, 2, PAN], BF16)
        eoc = [sb(f"eoc{i}", [128, PAN], BF16) for i in range(2)]
        qrot = sb("qrot", [128, PAN], BF16)

        ps = [es.enter_context(nc.psum_tensor(f"ps{i}", [128, 512], F32)) for i in range(8)]

        S = {}
        for name in ("dql", "dqs", "dqg", "dst", "tmm", "vev", "sev", "gev", "bar"):
            S[name] = SemC(es.enter_context(nc.semaphore(name)))
        block = es.enter_context(nc.Block())
        NBAR = [0]

        def barrier():
            NBAR[0] += 1
            tgt = 4 * NBAR[0]

            @block.tensor
            def _(t):
                t.nop().then_inc(S["bar"].h, 1)
                t.wait_ge(S["bar"].h, tgt)

            @block.vector
            def _(v):
                v.engine_nop().then_inc(S["bar"].h, 1)
                v.wait_ge(S["bar"].h, tgt)

            @block.scalar
            def _(s):
                s.copy(out=barsc[0:1, 0:1],
                       in_=fconst[0:1, 0:1]).then_inc(S["bar"].h, 1)
                s.wait_ge(S["bar"].h, tgt)

            @block.gpsimd
            def _(g):
                g.engine_nop().then_inc(S["bar"].h, 1)
                g.wait_ge(S["bar"].h, tgt)

            S["bar"].bump(4)

        def sync_wait_bar(sync):
            sync.wait_ge(S["bar"].h, 4 * NBAR[0])

        def serialize():
            tv, sv, vv = S["tmm"].n, S["sev"].n, S["vev"].n

            @block.tensor
            def _(t):
                t.wait_ge(S["vev"].h, vv)
                t.wait_ge(S["sev"].h, sv)

            @block.vector
            def _(v):
                v.wait_ge(S["tmm"].h, tv)
                v.wait_ge(S["sev"].h, sv)

            @block.scalar
            def _(s):
                s.wait_ge(S["tmm"].h, tv)
                s.wait_ge(S["vev"].h, vv)

        def gemm_pump_mm(n_oc, w_list, nd, moving_fn, ntt, ev, ev0, pre=None):
            dqg0 = S["dqg"].n
            tmm0 = S["tmm"].n

            @block.gpsimd
            def _(g):
                for j in range(n_oc):
                    if j >= 2:
                        g.wait_ge(S["tmm"].h, tmm0 + (j - 1) * ntt)
                    g.dma_start(out=wb[j % 2][:, 0:nd, :],
                                in_=w_list[j]).then_inc(S["dqg"].h, 16)
                    S["dqg"].bump(16)

            @block.tensor
            def _(t):
                if pre is not None:
                    pre(t)
                for j in range(n_oc):
                    t.wait_ge(S["dqg"].h, dqg0 + (j + 1) * 16)
                    if j >= 2:
                        t.wait_ge(ev.h, ev0 + (j - 1) * ntt)
                    for d in range(nd):
                        for tt in range(ntt):
                            ins = t.matmul(ps[(j % 2) * ntt + tt][:, 0:512],
                                           wb[j % 2][:, d, :], moving_fn(d, tt),
                                           start=(d == 0), stop=(d == nd - 1))
                            if d == nd - 1:
                                ins.then_inc(S["tmm"].h, 1)
                                S["tmm"].bump(1)
            return tmm0

        @block.sync
        def _(sync):
            for dst_sb, src in (
                (identb, identb_h), (identf, identf_h), (maskT, maskT_h),
                (ones3b, ones3b_h), (nones3b, nones3b_h), (ones1f, ones1f_h),
                (dtb, dtb_h), (alog, alog_h), (gnw, gnw_h),
                (coef, coef_h), (scal, scal_h), (bias_sb, bias_t),
                (fconst, fconst_h),
            ):
                sync.dma_start(out=dst_sb[:], in_=src[:]).then_inc(S["dql"].h, 16)
                S["dql"].bump(16)
            for i in range(2):
                sync.dma_start(out=sel_sb[:, i, :], in_=sel_h[i]).then_inc(S["dql"].h, 16)
                S["dql"].bump(16)
        s0_ld = S["dql"].n

        @block.gpsimd
        def _(g):
            g.dma_start(out=win_b[:], in_=win_t[:]).then_inc(S["dqg"].h, 16)
            S["dqg"].bump(16)

        vev0 = S["vev"].n

        @block.vector
        def _(v):
            v.wait_ge(S["dql"].h, s0_ld)
            v.memset(state[:], 0.0)
            v.memset(cumA[:, 0:64], 0.0)
            v.memset(cumB[:, 0:64], 0.0)
            v.tensor_mul(w16[:], coef[:], scal[:]).then_inc(S["vev"].h, 1)
            S["vev"].bump(1)

        @block.scalar
        def _(s):
            s.wait_ge(S["dql"].h, s0_ld)
            s.activation(out=acol[:], in_=alog[:], func=AF.Exp)
            s.mul(acol[:], acol[:], -1.0).then_inc(S["sev"].h, 1)
            S["sev"].bump(1)

        tmm0c = S["tmm"].n

        @block.tensor
        def _(t):
            t.wait_ge(S["vev"].h, vev0 + 1)
            for i in range(2):
                if i == 1:
                    t.wait_ge(S["vev"].h, vev0 + 2)
                t.matmul(ps[6][:, 0:1], sel_sb[:, i, :], w16[:],
                         start=True, stop=True).then_inc(S["tmm"].h, 1)
                S["tmm"].bump(1)

        @block.vector
        def _(v):
            for i in range(2):
                v.wait_ge(S["tmm"].h, tmm0c + i + 1)
                v.tensor_scalar(out=wcol[:, i:i + 1], in0=ps[6][:, 0:1], scalar1=0.0,
                                scalar2=None, op0=ALU.add).then_inc(S["vev"].h, 1)
                S["vev"].bump(1)

        with nc.sbuf_tensor("xtk0", [128, HID], F32) as xtk0, \
             nc.sbuf_tensor("xtk1", [128, HID], F32) as xtk1:
            xtk = [xtk0, xtk1]
            dql1 = S["dql"].n
            sev1 = S["sev"].n
            vev1 = S["vev"].n

            @block.sync
            def _(sync):
                for c in range(32):
                    if c >= 2:
                        sync.wait_ge(S["sev"].h, sev1 + c - 1)
                    sync.dma_start(out=xtk[c % 2][:],
                                   in_=x_tok[c * 128:(c + 1) * 128, :]).then_inc(S["dql"].h, 16)
                    S["dql"].bump(16)

            @block.vector
            def _(v):
                for c in range(32):
                    v.wait_ge(S["dql"].h, dql1 + (c + 1) * 16)
                    v.reduce_sum(st_sum[:, c:c + 1], xtk[c % 2][:],
                                 axis=mybir.AxisListType.X).then_inc(S["vev"].h, 1)
                    S["vev"].bump(1)

            @block.scalar
            def _(s):
                for c in range(32):
                    s.wait_ge(S["vev"].h, vev1 + c + 1)
                    s.activation(out=xtk[c % 2][:], in_=xtk[c % 2][:], func=AF.Square,
                                 accum_out=st_ss[:, c:c + 1]).then_inc(S["sev"].h, 1)
                    S["sev"].bump(1)

            sev1b = S["sev"].n
            vev1b = S["vev"].n

            @block.vector
            def _(v):
                v.wait_ge(S["sev"].h, sev1b)
                v.tensor_scalar(out=mean_cols[:], in0=st_sum[:], scalar1=1.0 / HID,
                                scalar2=None, op0=ALU.mult)
                v.tensor_scalar(out=st_ss[:], in0=st_ss[:], scalar1=1.0 / HID,
                                scalar2=None, op0=ALU.mult)
                v.tensor_mul(r_cols[:], mean_cols[:], mean_cols[:])
                v.tensor_sub(r_cols[:], st_ss[:], r_cols[:]).then_inc(S["vev"].h, 1)
                S["vev"].bump(1)

            @block.scalar
            def _(s):
                s.wait_ge(S["vev"].h, vev1b + 1)
                s.activation(out=r_cols[:], in_=r_cols[:], func=AF.Ln, bias=LN_EPS)
                s.activation(out=r_cols[:], in_=r_cols[:], func=AF.Exp,
                             scale=-0.5).then_inc(S["sev"].h, 1)
                S["sev"].bump(1)

            barrier()

        for pn in range(npan):
            own = pn >= 2
            pbase = pn * PAN
            opn = pn - 2
            obase = opn * PAN if own else 0

            @block.sync
            def _(sync):
                sync_wait_bar(sync)
                sync.dma_start(out=cos_pan[:], in_=cos_h[:, pbase:pbase + PAN]).then_inc(S["dql"].h, 16)
                sync.dma_start(out=sin_pan[:], in_=sin_h[:, pbase:pbase + PAN]).then_inc(S["dql"].h, 16)
                S["dql"].bump(32)
            rope_ld = S["dql"].n

            with nc.sbuf_tensor(f"xn_sb{pn}", [128, DC, PAN], BF16) as xn_sb:
                if level < 2:
                    continue
                with nc.sbuf_tensor(f"mfl{pn}", [1, PAN], F32) as mfl, \
                     nc.sbuf_tensor(f"rfl{pn}", [1, PAN], F32) as rfl, \
                     nc.sbuf_tensor(f"mrep{pn}", [128, PAN], BF16) as mrep, \
                     nc.sbuf_tensor(f"rrep{pn}", [128, PAN], F32) as rrep, \
                     nc.sbuf_tensor(f"xf0_{pn}", [128, 512], F32) as xf0, \
                     nc.sbuf_tensor(f"xf1_{pn}", [128, 512], F32) as xf1:
                    xf_b = [xf0, xf1]
                    dqgf = S["dqg"].n

                    @block.gpsimd
                    def _(g):
                        for c in range(NPC):
                            cg = pn * NPC + c
                            g.dma_start(out=mfl[0:1, c * 128:(c + 1) * 128],
                                        in_=mean_cols[:, cg:cg + 1]).then_inc(S["dqg"].h, 16)
                            g.dma_start(out=rfl[0:1, c * 128:(c + 1) * 128],
                                        in_=r_cols[:, cg:cg + 1]).then_inc(S["dqg"].h, 16)
                            S["dqg"].bump(32)
                    fl_done = S["dqg"].n
                    tmm1 = S["tmm"].n
                    vevr = S["vev"].n

                    @block.tensor
                    def _(t):
                        t.wait_ge(S["dqg"].h, fl_done)
                        for tt in range(NTT):
                            if tt == 1:
                                t.wait_ge(S["vev"].h, vevr + 2)
                            t.matmul(ps[6][:, 0:512], ones1f[:],
                                     rfl[0:1, tt * 512:(tt + 1) * 512],
                                     start=True, stop=True)
                            t.matmul(ps[7][:, 0:512], ones1f[:],
                                     mfl[0:1, tt * 512:(tt + 1) * 512],
                                     start=True, stop=True).then_inc(S["tmm"].h, 1)
                            S["tmm"].bump(1)

                    @block.vector
                    def _(v):
                        for tt in range(NTT):
                            v.wait_ge(S["tmm"].h, tmm1 + tt + 1)
                            v.tensor_copy(rrep[:, tt * 512:(tt + 1) * 512], ps[6][:, 0:512])
                            v.tensor_copy(mrep[:, tt * 512:(tt + 1) * 512],
                                          ps[7][:, 0:512]).then_inc(S["vev"].h, 2)
                            S["vev"].bump(2)
                    rep_done = S["vev"].n

                    dqlx = S["dql"].n
                    gevx = S["gev"].n
                    vevx = S["vev"].n

                    @block.sync
                    def _(sync):
                        sync_wait_bar(sync)
                        for i in range(DC * 2):
                            if i >= 2:
                                sync.wait_ge(S["gev"].h, gevx + i - 1)
                            d, half = i // 2, i % 2
                            sync.dma_start(out=xf_b[i % 2][:],
                                           in_=xT[:, d, pbase + half * 512:pbase + (half + 1) * 512]
                                           ).then_inc(S["dql"].h, 16)
                            S["dql"].bump(16)

                    @block.vector
                    def _(v):
                        v.wait_ge(S["vev"].h, rep_done)
                        for i in range(DC * 2):
                            half = i % 2
                            v.wait_ge(S["dql"].h, dqlx + (i + 1) * 16)
                            v.tensor_sub(xf_b[i % 2][:], xf_b[i % 2][:],
                                         mrep[:, half * 512:(half + 1) * 512]).then_inc(S["vev"].h, 1)
                            S["vev"].bump(1)

                    @block.gpsimd
                    def _(g):
                        for i in range(DC * 2):
                            d, half = i // 2, i % 2
                            g.wait_ge(S["vev"].h, vevx + i + 1)
                            g.tensor_mul(xn_sb[:, d, half * 512:(half + 1) * 512],
                                         xf_b[i % 2][:],
                                         rrep[:, half * 512:(half + 1) * 512]).then_inc(S["gev"].h, 1)
                            S["gev"].bump(1)

                    barrier()

                if level < 3:
                    continue
                def p2_gemm_done(j):
                    return tmm2 + (2 * (j + 1) if j <= 8 else 10 * j - 70)

                dqg2 = S["dqg"].n
                tmm2 = S["tmm"].n
                vev2 = S["vev"].n
                sevt = S["sev"].n

                @block.gpsimd
                def _(g):
                    for j in range(16):
                        if j >= 2:
                            g.wait_ge(S["tmm"].h, p2_gemm_done(j - 2))
                        g.dma_start(out=wb[j % 2][:, 0:DC, :],
                                    in_=wkv_t[j]).then_inc(S["dqg"].h, 16)
                        S["dqg"].bump(16)

                @block.tensor
                def _(t):
                    def emit_tr(i):
                        for c in range(NPC):
                            k = (i - 8) * NPC + c
                            if k >= 2:
                                t.wait_ge(S["sev"].h, sevt + k - 1)
                            t.transpose(ps[6 + k % 2].bitcast(BF16)[:, 0:128],
                                        eoc[i % 2][:, c * 128:(c + 1) * 128],
                                        identb[:]).then_inc(S["tmm"].h, 1)
                            S["tmm"].bump(1)

                    for j in range(16):
                        t.wait_ge(S["dqg"].h, dqg2 + (j + 1) * 16)
                        if j >= 2:
                            t.wait_ge(S["vev"].h, vev2 + (j - 1) * NTT)
                        for d in range(DC):
                            for tt in range(NTT):
                                ins = t.matmul(ps[(j % 2) * NTT + tt][:, 0:512],
                                               wb[j % 2][:, d, :],
                                               xn_sb[:, d, tt * 512:(tt + 1) * 512],
                                               start=(d == 0), stop=(d == DC - 1))
                                if d == DC - 1:
                                    ins.then_inc(S["tmm"].h, 1)
                                    S["tmm"].bump(1)
                        if j - 1 >= 8:
                            t.wait_ge(S["vev"].h, vev2 + j * NTT)
                            emit_tr(j - 1)
                    t.wait_ge(S["vev"].h, vev2 + 16 * NTT)
                    emit_tr(15)
                    for tt in range(NTT):
                        ins = None
                        for d in range(DC):
                            ins = t.matmul(ps[4 + tt][0:32, 0:512], win_b[:, d, :],
                                           xn_sb[:, d, tt * 512:(tt + 1) * 512],
                                           start=(d == 0), stop=(d == DC - 1))
                        ins.then_inc(S["tmm"].h, 1)
                        S["tmm"].bump(1)
                dt_tmm = S["tmm"].n - 2

                @block.vector
                def _(v):
                    for j in range(16):
                        v.wait_ge(S["tmm"].h, p2_gemm_done(j))
                        if j == 0:
                            v.wait_ge(S["dql"].h, rope_ld)
                        buf = eoc[j % 2]
                        for tt in range(NTT):
                            v.tensor_copy(buf[:, tt * 512:(tt + 1) * 512],
                                          ps[(j % 2) * NTT + tt][:, 0:512])
                        if j < 8:
                            v.tensor_scalar(out=qrot[0:64, :], in0=buf[64:128, :],
                                            scalar1=-1.0, scalar2=None, op0=ALU.mult)
                            v.tensor_copy(qrot[64:128, :], buf[0:64, :])
                            v.tensor_tensor(out=qrot[:], in0=qrot[:], in1=sin_pan[:], op=ALU.mult)
                            v.tensor_tensor(out=buf[:], in0=buf[:], in1=cos_pan[:], op=ALU.mult)
                            v.tensor_add(buf[:], buf[:], qrot[:])
                            v.tensor_copy(krope[:, j, :], buf[:])
                        v.engine_nop().then_inc(S["vev"].h, NTT)
                        S["vev"].bump(NTT)

                @block.scalar
                def _(s):
                    for k in range(64):
                        i, c = 8 + k // NPC, k % NPC
                        if i < 15:
                            w = p2_gemm_done(i + 1) + c + 1
                        else:
                            w = tmm2 + 88 + c + 1
                        s.wait_ge(S["tmm"].h, w)
                        s.copy(out=vt_sb[:, c, i % 8, :],
                               in_=ps[6 + k % 2].bitcast(BF16)[:, 0:128]).then_inc(S["sev"].h, 1)
                        S["sev"].bump(1)

                vevdt = S["vev"].n

                @block.vector
                def _(v):
                    for tt in range(NTT):
                        v.wait_ge(S["tmm"].h, dt_tmm + tt + 1)
                        v.tensor_scalar(out=dt_row[:, tt * 512:(tt + 1) * 512],
                                        in0=ps[4 + tt][0:32, 0:512], scalar1=dtb[:, 0:1],
                                        scalar2=None, op0=ALU.add)
                    v.engine_nop().then_inc(S["vev"].h, 1)
                    S["vev"].bump(1)

                sevdt = S["sev"].n

                @block.scalar
                def _(s):
                    s.wait_ge(S["vev"].h, vevdt + 1)
                    s.activation(out=g_row[:], in_=dt_row[:], func=AF.Abs)
                    s.activation(out=g_row[:], in_=g_row[:], func=AF.Exp, scale=-1.0)
                    s.activation(out=g_row[:], in_=g_row[:], func=AF.Ln, bias=1.0).then_inc(S["sev"].h, 1)
                    S["sev"].bump(1)

                @block.vector
                def _(v):
                    v.wait_ge(S["sev"].h, sevdt + 1)
                    v.scalar_tensor_tensor(out=dt_row[:], in0=dt_row[:], scalar=0.0,
                                           in1=g_row[:], op0=ALU.max, op1=ALU.add)
                    v.tensor_scalar(out=g_row[:], in0=dt_row[:], scalar1=acol[:, 0:1],
                                    scalar2=None, op0=ALU.mult)
                    for c in range(NPC):
                        gsl = g_row[:, c * 128:(c + 1) * 128]
                        v.tensor_scalar(out=cumA[:, 64:192], in0=gsl,
                                        scalar1=0.0, scalar2=None, op0=ALU.add)
                        cur, nxt = cumA, cumB
                        for sft in (1, 2, 4, 8, 16, 32):
                            v.tensor_add(nxt[:, 64:192], cur[:, 64:192],
                                         cur[:, 64 - sft:192 - sft])
                            cur, nxt = nxt, cur
                        v.tensor_add(G_row[:, c * 128:(c + 1) * 128],
                                     cur[:, 64:192], cur[:, 0:128])
                    v.tensor_scalar(out=spl_hi[:], in0=G_row[:], scalar1=0.0,
                                    scalar2=None, op0=ALU.add)
                    v.tensor_sub(g_row[:], G_row[:], spl_hi[:])
                    v.tensor_scalar(out=spl_mid[:], in0=g_row[:], scalar1=0.0,
                                    scalar2=None, op0=ALU.add)
                    v.tensor_sub(g_row[:], g_row[:], spl_mid[:])
                    v.tensor_scalar(out=spl_lo[:], in0=g_row[:], scalar1=0.0,
                                    scalar2=None, op0=ALU.add)
                    v.engine_nop().then_inc(S["vev"].h, 1)
                    S["vev"].bump(1)

                barrier()

                if own and level >= 4:
                    evq = S["vev"].n
                    q_tmm0 = gemm_pump_mm(32, [wq_t[i] for i in range(32)], DC,
                                          lambda d, tt: xn_sb[:, d, tt * 512:(tt + 1) * 512],
                                          NTT, S["vev"], evq)
                    dstq = S["dst"].n

                    @block.vector
                    def _(v):
                        for j in range(32):
                            v.wait_ge(S["tmm"].h, q_tmm0 + (j + 1) * NTT)
                            if j >= 2:
                                v.wait_ge(S["dst"].h, dstq + (j - 1) * 16)
                            buf = eoc[j % 2]
                            for tt in range(NTT):
                                v.tensor_copy(buf[:, tt * 512:(tt + 1) * 512],
                                              ps[(j % 2) * NTT + tt][:, 0:512])
                            v.tensor_scalar(out=qrot[0:64, :], in0=buf[64:128, :],
                                            scalar1=-1.0, scalar2=None, op0=ALU.mult)
                            v.tensor_copy(qrot[64:128, :], buf[0:64, :])
                            v.tensor_tensor(out=qrot[:], in0=qrot[:], in1=sin_pan[:], op=ALU.mult)
                            v.tensor_tensor(out=buf[:], in0=buf[:], in1=cos_pan[:], op=ALU.mult)
                            v.tensor_add(buf[:], buf[:], qrot[:])
                            v.engine_nop().then_inc(S["vev"].h, NTT)
                            S["vev"].bump(NTT)

                    @block.sync
                    def _(sync):
                        for j in range(32):
                            sync.wait_ge(S["vev"].h, evq + (j + 1) * NTT)
                            sync.dma_start(out=q_sp[opn, j], in_=eoc[j % 2][:]).then_inc(S["dst"].h, 16)
                            S["dst"].bump(16)

                    barrier()

                    if level < 5:
                        continue
                    evg = S["sev"].n
                    g_tmm0 = gemm_pump_mm(32, [wg_t[i] for i in range(32)], DC,
                                          lambda d, tt: xn_sb[:, d, tt * 512:(tt + 1) * 512],
                                          NTT, S["sev"], evg)
                    dstg = S["dst"].n

                    @block.scalar
                    def _(s):
                        for j in range(32):
                            if j >= 2:
                                s.wait_ge(S["dst"].h, dstg + (j - 1) * 16)
                            for tt in range(NTT):
                                s.wait_ge(S["tmm"].h, g_tmm0 + j * NTT + tt + 1)
                                s.activation(out=eoc[j % 2][:, tt * 512:(tt + 1) * 512],
                                             in_=ps[(j % 2) * NTT + tt][:, 0:512],
                                             func=AF.Silu).then_inc(S["sev"].h, 1)
                                S["sev"].bump(1)

                    @block.sync
                    def _(sync):
                        for j in range(32):
                            sync.wait_ge(S["sev"].h, evg + (j + 1) * NTT)
                            sync.dma_start(out=gate_sp[opn, j], in_=eoc[j % 2][:]).then_inc(S["dst"].h, 16)
                            S["dst"].bump(16)

                    barrier()

                    if level < 6:
                        continue
                    evxa = S["vev"].n
                    xa_tmm0 = gemm_pump_mm(2, [a_t[i] for i in range(2)], DC,
                                           lambda d, tt: xn_sb[:, d, tt * 512:(tt + 1) * 512],
                                           NTT, S["vev"], evxa)

                    @block.vector
                    def _(v):
                        for j in range(2):
                            for tt in range(NTT):
                                v.wait_ge(S["tmm"].h, xa_tmm0 + j * NTT + tt + 1)
                                v.tensor_scalar(out=xa_sb[:, j, tt * 512:(tt + 1) * 512],
                                                in0=ps[(j % 2) * NTT + tt][:, 0:512],
                                                scalar1=wcol[:, j:j + 1], scalar2=None,
                                                op0=ALU.mult).then_inc(S["vev"].h, 1)
                                S["vev"].bump(1)

                    barrier()

                    if level < 7:
                        continue
                    def moving_ffn(d, tt):
                        if d < DC:
                            return xn_sb[:, d, tt * 512:(tt + 1) * 512]
                        return xa_sb[:, d - DC, tt * 512:(tt + 1) * 512]

                    evf = S["sev"].n
                    f_tmm0 = gemm_pump_mm(32, [wfc_t[i] for i in range(32)], DC + 2,
                                          moving_ffn, NTT, S["sev"], evf)
                    dstf = S["dst"].n

                    @block.scalar
                    def _(s):
                        for j in range(32):
                            if j >= 2:
                                s.wait_ge(S["dst"].h, dstf + (j - 1) * 16)
                            for tt in range(NTT):
                                s.wait_ge(S["tmm"].h, f_tmm0 + j * NTT + tt + 1)
                                s.activation(out=eoc[j % 2][:, tt * 512:(tt + 1) * 512],
                                             in_=ps[(j % 2) * NTT + tt][:, 0:512],
                                             func=AF.Silu,
                                             bias=bias_sb[:, j:j + 1]).then_inc(S["sev"].h, 1)
                                S["sev"].bump(1)

                    @block.sync
                    def _(sync):
                        for j in range(32):
                            sync.wait_ge(S["sev"].h, evf + (j + 1) * NTT)
                            sync.dma_start(out=g_sp[opn, j], in_=eoc[j % 2][:]).then_inc(S["dst"].h, 16)
                            S["dst"].bump(16)

                    barrier()

            if level < 8:
                continue
            with nc.sbuf_tensor(f"ybuf{pn}", [128, 32, HPAN], BF16) as ybuf:
                hp_list = (0, 1) if own else (0,)
                for hp in hp_list:
                    hbase = hp * HPAN
                    if own:
                        dstq_done = S["dst"].n

                        @block.sync
                        def _(sync):
                            sync_wait_bar(sync)
                            sync.wait_ge(S["dst"].h, dstq_done)
                            for j in range(32):
                                sync.dma_start(out=ybuf[:, j, :],
                                               in_=q_sp[opn, j, :, hbase:hbase + HPAN]
                                               ).then_inc(S["dql"].h, 16)
                                S["dql"].bump(16)
                    qld_done = S["dql"].n

                    with ExitStack() as p7es:
                        def p7sb(name, shape, dt):
                            return p7es.enter_context(
                                nc.sbuf_tensor(f"{name}_{pn}_{hp}", shape, dt))
                        spl3f = p7sb("spl3f", [3, 2, 4096], BF16)
                        glf = p7sb("glf", [1, 64], F32)
                        ktc = p7sb("ktc", [128, KVH, 128], BF16)
                        dt_cT = p7sb("dt_cT", [128, 32], F32)
                        G_cT = p7sb("G_cT", [128, 32], F32)
                        w_cT = p7sb("w_cT", [128, 32], F32)
                        dtw_cT = p7sb("dtw_cT", [128, 32], F32)
                        eG_cT = p7sb("eG_cT", [128, 32], F32)
                        cdec = p7sb("cdec", [128, 32], F32)
                        xc4 = p7sb("xc4", [128, 4, 128], BF16)
                        xcw4 = p7sb("xcw4", [128, 4, 128], BF16)
                        pv4 = p7sb("pv4", [128, 4, 128], BF16)
                        mt_s = p7sb("mt_s", [128, 512], F32)
                        et_s = p7sb("et_s", [128, 512], BF16)
                        std_s = p7sb("std_s", [128, 512], BF16)
                        t1_s = p7sb("t1_s", [128, 512], F32)
                        yf_s = p7sb("yf_s", [128, 512], F32)
                        ss4 = p7sb("ss4", [128, 4], F32)
                        rinv4 = p7sb("rinv4", [128, 4], F32)
                        yn4 = p7sb("yn4", [128, 4, 128], BF16)

                        def emit_flatten(g, c):
                            g.dma_start(out=glf[0:1, (c % 2) * 32:(c % 2) * 32 + 32],
                                        in_=G_row[:, c * 128 + 127:c * 128 + 128]
                                        ).then_inc(S["dqg"].h, 16)
                            S["dqg"].bump(16)
                            if own:
                                for si, spl in enumerate((spl_hi, spl_mid, spl_lo)):
                                    g.dma_start(out=spl3f[si:si + 1, c % 2, :],
                                                in_=spl[:, c * 128:(c + 1) * 128]
                                                ).then_inc(S["dqg"].h, 16)
                                    S["dqg"].bump(16)

                        chunks = list(range(hp * 4, hp * 4 + 4)) if own else list(range(p7nc))
                        flat_cnt = {}
                        chunk_end_tmm = {}
                        for idx, c in enumerate(chunks):
                            first = (idx == 0)
                            nxt = chunks[idx + 1] if idx + 1 < len(chunks) else None

                            @block.gpsimd
                            def _(g):
                                if first:
                                    emit_flatten(g, c)
                                    flat_cnt[c] = S["dqg"].n
                                if nxt is not None:
                                    if not first:
                                        g.wait_ge(S["tmm"].h, chunk_end_tmm[c - 1])
                                    emit_flatten(g, nxt)
                                    flat_cnt[nxt] = S["dqg"].n
                            this_flat = flat_cnt[c]

                            tmmP = S["tmm"].n
                            vevP = S["vev"].n
                            sevP = S["sev"].n

                            @block.tensor
                            def _(t):
                                if p7pre >= 1:
                                    for kk in range(4):
                                        ins = t.transpose(ps[3 + kk].bitcast(BF16)[:, 0:128],
                                                          krope[:, kk, c * 128:(c + 1) * 128],
                                                          identb[:])
                                    ins.then_inc(S["tmm"].h, 1)
                                    S["tmm"].bump(1)
                                if p7v4 == 2 and p7pre >= 4:
                                    for kk in range(4):
                                        ins = t.transpose(ps[3 + kk].bitcast(BF16)[:, 128:256],
                                                          krope[:, 4 + kk, c * 128:(c + 1) * 128],
                                                          identb[:])
                                    ins.then_inc(S["tmm"].h, 1)
                                    S["tmm"].bump(1)
                                if p7pre >= 2:
                                    t.transpose(ps[7][:, 0:32], dt_row[0:32, c * 128:(c + 1) * 128],
                                                identf[0:32, 0:32]).then_inc(S["tmm"].h, 1)
                                    S["tmm"].bump(1)
                                if p7pre >= 3:
                                    t.wait_ge(S["vev"].h, vevP + 1)
                                    t.transpose(ps[7][:, 0:32], G_row[0:32, c * 128:(c + 1) * 128],
                                                identf[0:32, 0:32]).then_inc(S["tmm"].h, 1)
                                    S["tmm"].bump(1)
                                if p7pre >= 4 and p7v4 != 2:
                                    t.wait_ge(S["sev"].h, sevP + 1)
                                    if p7v4 == 0:
                                        for kk in range(4):
                                            ins = t.transpose(ps[3 + kk].bitcast(BF16)[:, 0:128],
                                                              krope[:, 4 + kk, c * 128:(c + 1) * 128],
                                                              identb[:])
                                    elif p7v4 == 1:
                                        for kk in range(4):
                                            ins = t.transpose(ps[3 + kk].bitcast(BF16)[:, 128:256],
                                                              krope[:, 4 + kk, c * 128:(c + 1) * 128],
                                                              identb[:])
                                    elif p7v4 == 3:
                                        for kk in range(4):
                                            ins = t.transpose(ps[3 + kk][:, 0:32],
                                                              G_row[0:32, c * 128:(c + 1) * 128],
                                                              identf[0:32, 0:32])
                                    ins.then_inc(S["tmm"].h, 1)
                                    S["tmm"].bump(1)
                                if p7pre >= 6:
                                    t.wait_ge(S["dqg"].h, this_flat)
                                    t.matmul(ps[2][:, 0:32], ones1f[:],
                                             glf[0:1, (c % 2) * 32:(c % 2) * 32 + 32],
                                             start=True, stop=True).then_inc(S["tmm"].h, 1)
                                    S["tmm"].bump(1)

                            @block.scalar
                            def _(s):
                                if p7pre >= 1:
                                    s.wait_ge(S["tmm"].h, tmmP + 1)
                                    for kk in range(4):
                                        ins = s.copy(out=ktc[:, kk, :],
                                                     in_=ps[3 + kk].bitcast(BF16)[:, 0:128])
                                    ins.then_inc(S["sev"].h, 1)
                                    S["sev"].bump(1)
                                if own:
                                    s.wait_ge(S["tmm"].h, tmmP + 3)
                                    s.activation(out=eG_cT[:], in_=ps[7][:, 0:32],
                                                 func=AF.Exp).then_inc(S["sev"].h, 1)
                                    S["sev"].bump(1)
                                if p7pre >= 5:
                                    s.wait_ge(S["tmm"].h, tmmP + 4)
                                    for kk in range(4):
                                        ins = s.copy(out=ktc[:, 4 + kk, :],
                                                     in_=ps[3 + kk].bitcast(BF16)[:, 0:128])
                                    ins.then_inc(S["sev"].h, 1)
                                    S["sev"].bump(1)
                                if p7pre >= 6:
                                    s.wait_ge(S["tmm"].h, tmmP + 5)
                                    s.activation(out=cdec[:], in_=ps[2][:, 0:32],
                                                 func=AF.Exp).then_inc(S["sev"].h, 1)
                                    S["sev"].bump(1)
                            sev_cdec = S["sev"].n

                            vevW = S["vev"].n

                            @block.vector
                            def _(v):
                                if p7pre >= 2:
                                    v.wait_ge(S["tmm"].h, tmmP + 2)
                                    v.tensor_scalar(out=dt_cT[:], in0=ps[7][:, 0:32],
                                                    scalar1=0.0, scalar2=None,
                                                    op0=ALU.add).then_inc(S["vev"].h, 1)
                                    S["vev"].bump(1)
                                if p7pre >= 3:
                                    if own:
                                        v.wait_ge(S["sev"].h, sevP + 2)
                                    else:
                                        v.wait_ge(S["tmm"].h, tmmP + 3)
                                    v.tensor_scalar(out=G_cT[:], in0=ps[7][:, 0:32],
                                                    scalar1=0.0, scalar2=None,
                                                    op0=ALU.add).then_inc(S["vev"].h, 1)
                                    S["vev"].bump(1)
                                if p7pre >= 6:
                                    v.wait_ge(S["sev"].h, sev_cdec)
                                    v.tensor_sub(w_cT[:], ps[2][:, 0:32], G_cT[:]).then_inc(S["vev"].h, 1)
                                    S["vev"].bump(1)

                            if p7pre >= 6:
                                @block.scalar
                                def _(s):
                                    s.wait_ge(S["vev"].h, vevW + 3)
                                    s.activation(out=w_cT[:], in_=w_cT[:], func=AF.Exp).then_inc(S["sev"].h, 1)
                                    S["sev"].bump(1)

                                @block.vector
                                def _(v):
                                    v.wait_ge(S["sev"].h, S["sev"].n)
                                    v.tensor_mul(dtw_cT[:], dt_cT[:], w_cT[:]).then_inc(S["vev"].h, 1)
                                    S["vev"].bump(1)

                            pre_s_done = S["sev"].n
                            pre_v_done = S["vev"].n
                            if p7ser >= 2:
                                serialize()

                            for gi in range(8 if own else p7ng):
                                h0 = 4 * gi
                                cc = c % 4
                                vev_g0 = S["vev"].n
                                tmm_g0 = S["tmm"].n
                                sev_g0 = S["sev"].n

                                @block.vector
                                def _(v):
                                    if gi == 0:
                                        v.wait_ge(S["sev"].h, pre_s_done)
                                        v.wait_ge(S["vev"].h, pre_v_done)
                                    if own:
                                        for hh in range(4):
                                            v.tensor_scalar(out=xc4[:, hh, :],
                                                            in0=vt_sb[:, c, gi, :],
                                                            scalar1=dt_cT[:, h0 + hh:h0 + hh + 1],
                                                            scalar2=None, op0=ALU.mult)
                                        for hh in range(4):
                                            v.tensor_scalar(out=xcw4[:, hh, :],
                                                            in0=xc4[:, hh, :],
                                                            scalar1=w_cT[:, h0 + hh:h0 + hh + 1],
                                                            scalar2=None, op0=ALU.mult)
                                    else:
                                        for hh in range(4):
                                            v.tensor_scalar(out=xcw4[:, hh, :],
                                                            in0=vt_sb[:, c, gi, :],
                                                            scalar1=dtw_cT[:, h0 + hh:h0 + hh + 1],
                                                            scalar2=None, op0=ALU.mult)
                                    v.engine_nop().then_inc(S["vev"].h, 1)
                                    S["vev"].bump(1)
                                    if own:
                                        v.tensor_copy(pv4[:], state[:, h0:h0 + 4, :]).then_inc(S["vev"].h, 1)
                                        S["vev"].bump(1)

                                @block.tensor
                                def _(t):
                                    t.wait_ge(S["vev"].h, vev_g0 + 1)
                                    t.matmul(ps[0][:, 0:512], ktc[:, gi, :],
                                             xcw4[:].rearrange("p h e -> p (h e)"),
                                             start=True, stop=True).then_inc(S["tmm"].h, 1)
                                    S["tmm"].bump(1)
                                    if own:
                                        if gi == 0:
                                            t.wait_ge(S["dql"].h, qld_done)
                                        for hh in range(4):
                                            ins = t.matmul(ps[1][:, hh * 128:(hh + 1) * 128],
                                                           krope[:, gi, c * 128:(c + 1) * 128],
                                                           ybuf[:, h0 + hh, cc * 128:(cc + 1) * 128],
                                                           start=True, stop=True)
                                        ins.then_inc(S["tmm"].h, 1)
                                        S["tmm"].bump(1)
                                        for hh in range(4):
                                            hc2 = (h0 + hh) * 128
                                            reg = ps[2][:, hh * 128:(hh + 1) * 128]
                                            t.matmul(reg, ones3b[:],
                                                     spl3f[:, c % 2, hc2:hc2 + 128],
                                                     start=True, stop=False)
                                            ins = t.matmul(reg,
                                                           spl3f[:, c % 2, hc2:hc2 + 128],
                                                           nones3b[:], start=False, stop=True)
                                        ins.then_inc(S["tmm"].h, 1)
                                        S["tmm"].bump(1)

                                if own:
                                    @block.vector
                                    def _(v):
                                        v.wait_ge(S["tmm"].h, tmm_g0 + 3)
                                        for hh in range(4):
                                            v.tensor_add(mt_s[:, hh * 128:(hh + 1) * 128],
                                                         ps[2][:, hh * 128:(hh + 1) * 128],
                                                         maskT[:])
                                        v.engine_nop().then_inc(S["vev"].h, 1)
                                        S["vev"].bump(1)

                                    @block.scalar
                                    def _(s):
                                        s.wait_ge(S["vev"].h, vev_g0 + 3)
                                        s.activation(out=et_s[:], in_=mt_s[:],
                                                     func=AF.Exp).then_inc(S["sev"].h, 1)
                                        S["sev"].bump(1)

                                    @block.vector
                                    def _(v):
                                        v.wait_ge(S["sev"].h, sev_g0 + 1)
                                        v.tensor_tensor(out=std_s[:], in0=ps[1][:, 0:512],
                                                        in1=et_s[:], op=ALU.mult).then_inc(S["vev"].h, 1)
                                        S["vev"].bump(1)

                                    par = gi % 2
                                    tmm_y0 = S["tmm"].n

                                    @block.tensor
                                    def _(t):
                                        t.wait_ge(S["vev"].h, vev_g0 + 4)
                                        for hh in range(4):
                                            t.matmul(ps[3 + par][:, hh * 128:(hh + 1) * 128],
                                                     std_s[:, hh * 128:(hh + 1) * 128],
                                                     xc4[:, hh, :], start=True, stop=True)
                                        for hh in range(4):
                                            ins = t.matmul(ps[5 + par][:, hh * 128:(hh + 1) * 128],
                                                           ybuf[:, h0 + hh, cc * 128:(cc + 1) * 128],
                                                           pv4[:, hh, :], start=True, stop=True)
                                        ins.then_inc(S["tmm"].h, 1)
                                        S["tmm"].bump(1)

                                    vev_y0 = S["vev"].n
                                    sev_y0 = S["sev"].n

                                    @block.vector
                                    def _(v):
                                        v.wait_ge(S["tmm"].h, tmm_y0 + 1)
                                        for hh in range(4):
                                            v.tensor_scalar(out=t1_s[:, hh * 128:(hh + 1) * 128],
                                                            in0=ps[5 + par][:, hh * 128:(hh + 1) * 128],
                                                            scalar1=eG_cT[:, h0 + hh:h0 + hh + 1],
                                                            scalar2=None, op0=ALU.mult)
                                        v.tensor_add(yf_s[:], t1_s[:],
                                                     ps[3 + par][:, 0:512]).then_inc(S["vev"].h, 1)
                                        S["vev"].bump(1)

                                    @block.scalar
                                    def _(s):
                                        s.wait_ge(S["vev"].h, vev_y0 + 1)
                                        for hh in range(4):
                                            s.activation(out=t1_s[:, hh * 128:(hh + 1) * 128],
                                                         in_=yf_s[:, hh * 128:(hh + 1) * 128],
                                                         func=AF.Square, accum_out=ss4[:, hh:hh + 1])
                                        s.activation(out=rinv4[:], in_=ss4[:], func=AF.Ln,
                                                     scale=1.0 / CH, bias=RMS_EPS)
                                        s.activation(out=rinv4[:], in_=rinv4[:], func=AF.Exp,
                                                     scale=-0.5).then_inc(S["sev"].h, 1)
                                        S["sev"].bump(1)

                                    tmm_tr0 = S["tmm"].n
                                    vev_n0 = S["vev"].n

                                    @block.vector
                                    def _(v):
                                        v.wait_ge(S["sev"].h, sev_y0 + 1)
                                        for hh in range(4):
                                            v.tensor_scalar(out=yn4[:, hh, :],
                                                            in0=yf_s[:, hh * 128:(hh + 1) * 128],
                                                            scalar1=rinv4[:, hh:hh + 1],
                                                            scalar2=None, op0=ALU.mult)
                                        v.engine_nop().then_inc(S["vev"].h, 1)
                                        S["vev"].bump(1)

                                    @block.tensor
                                    def _(t):
                                        t.wait_ge(S["vev"].h, vev_n0 + 1)
                                        for hh in range(4):
                                            ins = t.transpose(ps[7].bitcast(BF16)[:, hh * 128:(hh + 1) * 128],
                                                              yn4[:, hh, :], identb[:])
                                        ins.then_inc(S["tmm"].h, 1)
                                        S["tmm"].bump(1)

                                    @block.vector
                                    def _(v):
                                        v.wait_ge(S["tmm"].h, tmm_tr0 + 1)
                                        v.tensor_scalar(
                                            out=ybuf[:, h0:h0 + 4, cc * 128:(cc + 1) * 128],
                                            in0=ps[7].bitcast(BF16)[:, 0:512].rearrange("p (h t) -> p h t", h=4),
                                            scalar1=gnw[:, 0:1], scalar2=None,
                                            op0=ALU.mult).then_inc(S["vev"].h, 1)
                                        S["vev"].bump(1)

                                @block.vector
                                def _(v):
                                    v.wait_ge(S["tmm"].h, tmm_g0 + 1)
                                    for hh in range(4):
                                        v.scalar_tensor_tensor(
                                            out=state[:, h0 + hh, :], in0=state[:, h0 + hh, :],
                                            scalar=cdec[:, h0 + hh:h0 + hh + 1],
                                            in1=ps[0][:, hh * 128:(hh + 1) * 128],
                                            op0=ALU.mult, op1=ALU.add)
                                    v.engine_nop().then_inc(S["vev"].h, 1)
                                    S["vev"].bump(1)
                                su_count = S["vev"].n

                                @block.tensor
                                def _(t):
                                    t.wait_ge(S["vev"].h, su_count)
                                if p7ser >= 1:
                                    serialize()
                                if dbg2 and own and pn == 2 and hp == 0 and c == 0 and gi == 0:
                                    serialize()
                                    tv, sv, vv = S["tmm"].n, S["sev"].n, S["vev"].n
                                    items = [("mt_s", mt_s), ("et_s", et_s),
                                             ("std_s", std_s), ("t1_s", t1_s),
                                             ("yf_s", yf_s), ("ss4", ss4),
                                             ("rinv4", rinv4), ("yn4", yn4),
                                             ("xc4", xc4), ("xcw4", xcw4),
                                             ("pv4", pv4), ("dt_cT", dt_cT),
                                             ("G_cT", G_cT), ("w_cT", w_cT),
                                             ("eG_cT", eG_cT), ("cdec", cdec),
                                             ("ktc", ktc), ("glf", glf),
                                             ("G_row", G_row), ("g_row", g_row),
                                             ("dt_row", dt_row), ("acol", acol),
                                             ("spl_hi", spl_hi), ("spl_mid", spl_mid),
                                             ("spl_lo", spl_lo)]

                                    @block.sync
                                    def _(sync):
                                        sync.wait_ge(S["tmm"].h, tv)
                                        sync.wait_ge(S["sev"].h, sv)
                                        sync.wait_ge(S["vev"].h, vv)
                                        for nm, src_sb in items:
                                            sync.dma_start(out=dbg3[nm][:],
                                                           in_=src_sb[:]).then_inc(S["dst"].h, 16)
                                            S["dst"].bump(16)
                                        sync.dma_start(out=dbg3["state4"][:],
                                                       in_=state[:, 0:4, :]).then_inc(S["dst"].h, 16)
                                        S["dst"].bump(16)
                                    dmp_done = S["dst"].n

                                    @block.tensor
                                    def _(t):
                                        t.wait_ge(S["dst"].h, dmp_done)

                                    @block.vector
                                    def _(v):
                                        v.wait_ge(S["dst"].h, dmp_done)

                                    @block.scalar
                                    def _(s):
                                        s.wait_ge(S["dst"].h, dmp_done)
                            chunk_end_tmm[c] = S["tmm"].n

                    barrier()

                    if dbg and own:
                        @block.sync
                        def _(sync):
                            sync_wait_bar(sync)
                            sync.dma_start(out=ydbg_h[opn, hp],
                                           in_=ybuf[:]).then_inc(S["dst"].h, 16)
                            S["dst"].bump(16)

                    if own and level >= 9:
                        dqlg = S["dql"].n
                        vevp8 = S["vev"].n

                        @block.sync
                        def _(sync):
                            sync_wait_bar(sync)
                            for j in range(32):
                                if j >= 2:
                                    sync.wait_ge(S["vev"].h, vevp8 + j - 1)
                                sync.dma_start(out=eoc[j % 2][:, 0:HPAN],
                                               in_=gate_sp[opn, j, :, hbase:hbase + HPAN]
                                               ).then_inc(S["dql"].h, 16)
                                S["dql"].bump(16)

                        @block.vector
                        def _(v):
                            for j in range(32):
                                v.wait_ge(S["dql"].h, dqlg + (j + 1) * 16)
                                v.tensor_mul(ybuf[:, j, :], ybuf[:, j, :],
                                             eoc[j % 2][:, 0:HPAN]).then_inc(S["vev"].h, 1)
                                S["vev"].bump(1)
                        gmul_done = S["vev"].n

                        def wo_pre(t):
                            t.wait_ge(S["vev"].h, gmul_done)

                        evw = S["vev"].n
                        wo_tmm0 = gemm_pump_mm(32, [wo_t[i] for i in range(32)], DC,
                                               lambda d, tt: ybuf[:, d, :],
                                               1, S["vev"], evw, pre=wo_pre)

                        with nc.sbuf_tensor(f"gb2a_{pn}_{hp}", [128, HPAN], BF16) as gb2a, \
                             nc.sbuf_tensor(f"gb2b_{pn}_{hp}", [128, HPAN], BF16) as gb2b, \
                             nc.sbuf_tensor(f"xra_{pn}_{hp}", [128, HPAN], F32) as xra, \
                             nc.sbuf_tensor(f"xrb_{pn}_{hp}", [128, HPAN], F32) as xrb, \
                             nc.sbuf_tensor(f"t1pa_{pn}_{hp}", [128, HPAN], F32) as t1pa, \
                             nc.sbuf_tensor(f"t1pb_{pn}_{hp}", [128, HPAN], F32) as t1pb:
                            gb2 = [gb2a, gb2b]
                            xr_b = [xra, xrb]
                            t1p = [t1pa, t1pb]
                            dqs8 = S["dqs"].n
                            dst_pre = S["dst"].n

                            @block.scalar
                            def _(s):
                                s.wait_ge(S["dst"].h, dst_pre)
                                for j in range(32):
                                    if j >= 2:
                                        s.wait_ge(S["vev"].h, evw + (j - 1))
                                    s.dma_start(out=gb2[j % 2][:],
                                                in_=g_sp[opn, j, :, hbase:hbase + HPAN]
                                                ).then_inc(S["dqs"].h, 16)
                                    s.dma_start(out=xr_b[j % 2][:],
                                                in_=xT[:, j, 2048 + obase + hbase:
                                                       2048 + obase + hbase + HPAN]
                                                ).then_inc(S["dqs"].h, 16)
                                    S["dqs"].bump(32)

                            dstw = S["dst"].n

                            @block.vector
                            def _(v):
                                for j in range(32):
                                    v.wait_ge(S["tmm"].h, wo_tmm0 + j + 1)
                                    v.wait_ge(S["dqs"].h, dqs8 + (j + 1) * 32)
                                    if j >= 2:
                                        v.wait_ge(S["dst"].h, dstw + (j - 1) * 16)
                                    v.tensor_tensor(out=t1p[j % 2][:], in0=ps[j % 2][:, 0:512],
                                                    in1=gb2[j % 2][:], op=ALU.mult)
                                    v.tensor_tensor(out=t1p[j % 2][:], in0=t1p[j % 2][:],
                                                    in1=xr_b[j % 2][:],
                                                    op=ALU.add).then_inc(S["vev"].h, 1)
                                    S["vev"].bump(1)

                            @block.sync
                            def _(sync):
                                for j in range(32):
                                    sync.wait_ge(S["vev"].h, evw + j + 1)
                                    sync.dma_start(out=out_t[j][:, obase + hbase:
                                                                obase + hbase + HPAN],
                                                   in_=t1p[j % 2][:]).then_inc(S["dst"].h, 16)
                                    S["dst"].bump(16)

                            barrier()

        if dbg:
            @block.sync
            def _(sync):
                sync.wait_ge(S["dst"].h, S["dst"].n)
                for i, src in enumerate((mean_cols, r_cols, st_sum, st_ss)):
                    sync.dma_start(out=dbg_h[:, i, :], in_=src[:]).then_inc(S["dst"].h, 16)
                    S["dst"].bump(16)
                for i, src in enumerate((dt_row, G_row)):
                    sync.dma_start(out=dbg2_h[:, i, :], in_=src[:]).then_inc(S["dst"].h, 16)
                    S["dst"].bump(16)

        @block.sync
        def _(sync):
            sync.wait_ge(S["dst"].h, S["dst"].n)

    return nc


def _tile4(W, noc):
    return np.ascontiguousarray(
        W.reshape(noc, 128, DC, 128).transpose(0, 3, 2, 1))


_NC_CACHE = [None]


def kernel(L0_attn_hidden_states, L0_attn_Wq, L0_attn_Wk, L0_attn_Wv, L0_attn_Wo,
           L0_attn_Wg, L0_attn_Win, L0_attn_dt_bias, L0_attn_A_log_bias,
           L0_attn_gnorm_w, L0_ffn_W, L0_ffn_bias, L0_ffn_A, L0_ffn_Bf,
           L0_ffn_coeffs, L0_ffn_scales):
    x = np.asarray(L0_attn_hidden_states, dtype=np.float32)
    Wq = np.asarray(L0_attn_Wq, np.float32)
    Wk = np.asarray(L0_attn_Wk, np.float32)
    Wv = np.asarray(L0_attn_Wv, np.float32)
    Wo = np.asarray(L0_attn_Wo, np.float32)
    Wg = np.asarray(L0_attn_Wg, np.float32)
    Win = np.asarray(L0_attn_Win, np.float32)
    Wf = np.asarray(L0_ffn_W, np.float32)
    A = np.asarray(L0_ffn_A, np.float32)
    Bf = np.asarray(L0_ffn_Bf, np.float32)

    wq_t = _tile4(Wq, 32)
    wkv_t = np.ascontiguousarray(
        np.concatenate([_tile4(Wk, 8), _tile4(Wv, 8)], axis=0))
    wg_t = _tile4(Wg, 32)
    wo_t = _tile4(Wo, 32)
    wf_t = _tile4(Wf, 32)
    A_all = A.reshape(256, HID)
    a_t = np.ascontiguousarray(A_all.reshape(2, 128, DC, 128).transpose(0, 3, 2, 1))
    C = np.ascontiguousarray(Bf.transpose(0, 2, 1).reshape(256, HID))
    c_t = np.ascontiguousarray(C.reshape(2, 128, 32, 128).transpose(2, 1, 0, 3))
    wfc_t = np.ascontiguousarray(np.concatenate([wf_t, c_t], axis=2))
    win_t = np.ascontiguousarray(Win.T.reshape(DC, 128, 32).transpose(1, 0, 2))
    bias_t = np.ascontiguousarray(np.asarray(L0_ffn_bias, np.float32).reshape(32, 128).T)
    dtb = np.asarray(L0_attn_dt_bias, np.float32).reshape(32, 1)
    alog = np.asarray(L0_attn_A_log_bias, np.float32).reshape(32, 1)
    gnw = np.asarray(L0_attn_gnorm_w, np.float32).reshape(128, 1)
    coef = np.asarray(L0_ffn_coeffs, np.float32).reshape(16, 1)
    scal = np.asarray(L0_ffn_scales, np.float32).reshape(16, 1)

    maskT = np.where(np.arange(128)[:, None] <= np.arange(128)[None, :], 0.0,
                     np.float32(BIG_NEG)).astype(np.float32)
    identb = np.eye(128, dtype=ml_dtypes.bfloat16)
    identf = np.eye(128, dtype=np.float32)
    ones3b = np.ones((3, 128), ml_dtypes.bfloat16)
    nones3b = -np.ones((3, 128), ml_dtypes.bfloat16)
    ones1f = np.ones((1, 128), np.float32)
    sel = np.zeros((2, 16, 128), np.float32)
    for ccc in range(2):
        for m in range(128):
            sel[ccc, (ccc * 128 + m) // 16, m] = 1.0

    inv_freq = (1.0 / ROPE_THETA ** (np.arange(0, 128, 2, dtype=np.float32) / 128.0))

    in_maps = []
    for core in range(8):
        b, half = core // 2, core % 2
        own = x[b, half * T_OWN:(half + 1) * T_OWN]
        if half == 1:
            prefix = x[b, 0:T_OWN]
        else:
            prefix = np.zeros_like(own)
        x_tok = np.ascontiguousarray(np.concatenate([prefix, own], axis=0))
        xTl = np.ascontiguousarray(x_tok.T.reshape(DC, 128, 4096).transpose(1, 0, 2))
        pos_pre = np.arange(0, T_OWN, dtype=np.float32)
        pos_own = np.arange(half * T_OWN, (half + 1) * T_OWN, dtype=np.float32)
        pos = np.concatenate([pos_pre, pos_own])
        ang = pos[None, :] * inv_freq[:, None]
        cos_t = np.ascontiguousarray(
            np.concatenate([np.cos(ang), np.cos(ang)], axis=0)).astype(ml_dtypes.bfloat16)
        sin_t = np.ascontiguousarray(
            np.concatenate([np.sin(ang), np.sin(ang)], axis=0)).astype(ml_dtypes.bfloat16)
        in_maps.append(dict(
            x_tok=x_tok, xT=xTl, wkv_t=wkv_t, wq_t=wq_t, wg_t=wg_t, wfc_t=wfc_t,
            wo_t=wo_t, a_t=a_t, win_t=win_t, bias_t=bias_t,
            dtb=dtb, alog=alog, gnw=gnw, coef=coef, scal=scal,
            cos_t=cos_t, sin_t=sin_t, maskT=maskT, identb=identb, identf=identf,
            ones3b=ones3b, nones3b=nones3b, ones1f=ones1f, sel=sel,
            fconst=np.tile(np.array([[LN_EPS, -0.5, 1.0 / CH, 1.0 / HID]], np.float32),
                           (128, 1)),
        ))

    if _NC_CACHE[0] is None:
        _NC_CACHE[0] = build_nc()
    nc = _NC_CACHE[0]
    res = run_bass_kernel_spmd(nc, in_maps, core_ids=list(range(8)))

    out = np.empty((4, 4096, HID), np.float32)
    for core in range(8):
        b, half = core // 2, core % 2
        shard = res.results[core]["out_t"]
        out[b, half * T_OWN:(half + 1) * T_OWN, :] = (
            np.transpose(shard, (2, 0, 1)).reshape(T_OWN, HID))
    return out


if __name__ == "__main__":
    import time
    t0 = time.time()
    nc = build_nc()
    print(f"build ok in {time.time()-t0:.1f}s")
